# version 18 (final)
# speedup vs baseline: 1.6749x; 1.0751x over previous
import numpy as np
import contextlib
import concourse.bass as bass
import concourse.mybir as mybir
from concourse.bass_utils import run_bass_kernel_spmd

F32 = mybir.dt.float32
BF16 = mybir.dt.bfloat16
AF = mybir.ActivationFunctionType
ALU = mybir.AluOpType
AX = mybir.AxisListType

P = 128
SPC = 2
SEQ = 2048
DM = 1024
NDC = DM // P
INNER = 4096
NIC = INNER // P
TA = 128
NCH = SEQ // TA
TS = 64
NSUB = TA // TS
NST = 16
DTR = 64
EPS = 1e-5
C32 = 1.5 * 2.0**23
C64 = 1.5 * 2.0**52
KMUL = float(np.float32((C64 - C32) * (2.0**40)))
NTT = SPC * NCH


class Buf:
    __slots__ = ("ap", "writer", "readers")

    def __init__(self, ap):
        self.ap = ap
        self.writer = None
        self.readers = []


NDSEM = 32


class Sched:
    ENG = ("t", "s", "v", "g")
    STREAMS = ("d", "t", "s", "v", "g")

    def __init__(self):
        self.nodes = []
        self.frontier = None

    def _deps(self, reads, writes):
        deps = []
        for b in reads:
            if b.writer is not None:
                deps.append(b.writer)
        for b in writes:
            deps.extend(b.readers)
            if b.writer is not None:
                deps.append(b.writer)
        if self.frontier is not None:
            deps.append(self.frontier)
        return deps

    def add(self, ek, buildfn, reads=(), writes=(), inc=True, cost=300.0):
        deps = self._deps(reads, writes)
        nid = len(self.nodes)
        self.nodes.append((ek, buildfn, float(cost), deps, False, 0.0))
        for b in reads:
            b.readers.append(nid)
        for b in writes:
            b.writer = nid
            b.readers = []
        return nid

    def add_dma(self, buildfn, reads=(), writes=(), xfer=2000.0):
        deps = self._deps(reads, writes)
        nid = len(self.nodes)
        self.nodes.append(("d", buildfn, 150.0, deps, True, float(xfer)))
        for b in reads:
            b.readers.append(nid)
        for b in writes:
            b.writer = nid
            b.readers = []
        return nid

    def barrier(self):
        deps = list(range(self.frontier + 1 if self.frontier is not None else 0,
                          len(self.nodes)))
        nid = len(self.nodes)
        self.nodes.append((None, None, 0.0, deps, False, 0.0))
        self.frontier = nid
        return nid

    def _schedule(self):
        import os as _os
        if _os.environ.get("K_NOSCHED"):
            order = {e: [] for e in self.STREAMS}
            for i, (ek, _, _, _, _, _) in enumerate(self.nodes):
                if ek is not None:
                    order[ek].append(i)
            self.est_span = 0.0
            return order
        import heapq
        n = len(self.nodes)
        succ = [[] for _ in range(n)]
        indeg = [0] * n
        for i, (_, _, _, deps, _, _) in enumerate(self.nodes):
            for d in set(deps):
                succ[d].append(i)
                indeg[i] += 1
        prio = [0.0] * n
        for i in range(n - 1, -1, -1):
            ek, _, cost, _, is_dma, xfer = self.nodes[i]
            best = 0.0
            for s_ in succ[i]:
                if prio[s_] > best:
                    best = prio[s_]
            prio[i] = best + cost + (xfer if is_dma else 0.0)
        ready = {e: [] for e in self.STREAMS}
        free = {e: 0.0 for e in self.STREAMS}
        fin = [0.0] * n
        depcnt = list(indeg)
        rdytime = [0.0] * n
        barrier_ready = []
        order = {e: [] for e in self.STREAMS}
        def push(i):
            ek = self.nodes[i][0]
            if ek is None:
                barrier_ready.append(i)
            else:
                heapq.heappush(ready[ek], (-prio[i], i))
        for i in range(n):
            if depcnt[i] == 0:
                push(i)
        ndone = 0
        while ndone < n:
            while barrier_ready:
                i = barrier_ready.pop()
                fin[i] = rdytime[i]
                ndone += 1
                for s_ in succ[i]:
                    depcnt[s_] -= 1
                    rdytime[s_] = max(rdytime[s_], fin[i])
                    if depcnt[s_] == 0:
                        push(s_)
                continue
            bestpick = None
            for e in self.STREAMS:
                while ready[e]:
                    i = ready[e][0][1]
                    start = max(free[e], rdytime[i])
                    if bestpick is None or start < bestpick[0] or \
                       (start == bestpick[0] and -ready[e][0][0] > bestpick[3]):
                        bestpick = (start, e, i, -ready[e][0][0])
                    break
            if bestpick is None:
                if barrier_ready:
                    continue
                raise RuntimeError("deadlock in scheduler")
            start, e, i, _ = bestpick
            import heapq as hq
            hq.heappop(ready[e])
            ek, _, cost, _, is_dma, xfer = self.nodes[i]
            end_issue = start + cost
            free[e] = end_issue
            fin[i] = end_issue + (xfer if is_dma else 0.0)
            order[e].append(i)
            ndone += 1
            for s_ in succ[i]:
                depcnt[s_] -= 1
                rdytime[s_] = max(rdytime[s_], fin[i])
                if depcnt[s_] == 0:
                    push(s_)
        self.est_span = max(fin) if n else 0.0
        if _os.environ.get("K_DEBUG"):
            cp = max(range(n), key=lambda i: prio[i])
            print("critical path (no-resource): %.2f ms" % (prio[cp] / 1e6))
            from collections import Counter
            hist = Counter()
            i = cp
            while True:
                ek, _, cost, _, is_dma, xfer = self.nodes[i]
                hist[(ek, cost, xfer if is_dma else 0)] += 1
                nxt = None
                for s_ in succ[i]:
                    if nxt is None or prio[s_] > prio[nxt]:
                        nxt = s_
                if nxt is None:
                    break
                i = nxt
            for kv, ct in hist.most_common(14):
                print("  CP node (eng=%s cost=%s xfer=%s): x%d  tot=%.2fms"
                      % (kv[0], kv[1], kv[2], ct, ct * (kv[1] + kv[2]) / 1e6))
        return order

    def emit(self, block, sems):
        order = self._schedule()
        semid = [None] * len(self.nodes)
        cnt = {e: 0 for e in self.ENG}
        dcnt = [0] * NDSEM
        dmagate = [None] * len(self.nodes)
        dnext = 0
        for e in self.ENG:
            for i in order[e]:
                cnt[e] += 1
                semid[i] = (e, cnt[e])
        for i in order["d"]:
            j = dnext
            dnext = (j + 1) % NDSEM
            if dcnt[j] > 0:
                dmagate[i] = (f"d{j}", 16 * dcnt[j])
            dcnt[j] += 1
            semid[i] = (f"d{j}", 16 * dcnt[j])
        self.final_dma = [(f"d{j}", 16 * dcnt[j]) for j in range(NDSEM) if dcnt[j]]
        vc = {e: {} for e in self.STREAMS}
        snap = {}
        emitted = {e: [] for e in self.STREAMS}
        pos = {e: 0 for e in self.STREAMS}
        nexti = {e: 0 for e in self.STREAMS}
        done = [False] * len(self.nodes)
        def deps_done(i):
            return all(done[d] or self.nodes[d][0] is None and _bar_done(d)
                       for d in self.nodes[i][3])
        def _bar_done(b):
            return all(done[d] for d in self.nodes[b][3])
        progress = True
        total = sum(len(order[e]) for e in self.STREAMS)
        processed = 0
        while processed < total:
            progress = False
            for e in self.STREAMS:
                while nexti[e] < len(order[e]):
                    i = order[e][nexti[e]]
                    if not deps_done(i):
                        break
                    waits = []
                    mydeps = []
                    for d in set(self.nodes[i][3]):
                        if self.nodes[d][0] is None:
                            mydeps.extend(self.nodes[d][3])
                        else:
                            mydeps.append(d)
                    for d in set(mydeps):
                        dk, val = semid[d]
                        if vc[e].get(dk, 0) >= val:
                            continue
                        waits.append((dk, val))
                        ps = snap.get((dk, val))
                        if ps:
                            for k2, v2 in ps.items():
                                if v2 > vc[e].get(k2, 0):
                                    vc[e][k2] = v2
                        if val > vc[e].get(dk, 0):
                            vc[e][dk] = val
                    if dmagate[i] is not None:
                        gk, gv = dmagate[i]
                        if vc[e].get(gk, 0) < gv:
                            waits.append((gk, gv))
                            vc[e][gk] = gv
                    key, val = semid[i]
                    s_ = dict(vc[e])
                    s_[key] = val
                    snap[(key, val)] = s_
                    emitted[e].append((waits, self.nodes[i][1], key))
                    done[i] = True
                    nexti[e] += 1
                    processed += 1
                    progress = True
            if not progress:
                raise RuntimeError("wait-computation stuck")
        for ek, attr in (("d", "sync"), ("t", "tensor"), ("s", "scalar"),
                         ("v", "vector"), ("g", "gpsimd")):
            oplist = emitted[ek]
            if not oplist:
                continue

            def section(eng, _ops=oplist):
                for (waits, buildfn, key) in _ops:
                    for (dk, val) in waits:
                        eng.wait_ge(sems[dk], val)
                    if buildfn is None:
                        continue
                    inst = buildfn(eng)
                    inst.then_inc(sems[key[0] if len(key) == 1 else key],
                                  16 if key.startswith("d") else 1)

            getattr(block, attr)(section)


def ln_stats(sc, stat, n, epsc=None):
    inv = 1.0 / n
    sc.add("v", lambda e: e.tensor_scalar(stat.ap[:, 2:3], stat.ap[:, 0:1],
                                          -inv, None, ALU.mult),
           reads=[stat], writes=[stat])
    sc.add("g", lambda e: e.tensor_tensor(stat.ap[:, 3:4], stat.ap[:, 2:3],
                                          stat.ap[:, 2:3], ALU.mult),
           reads=[stat], writes=[stat])
    sc.add("g", lambda e: e.tensor_scalar(stat.ap[:, 1:2], stat.ap[:, 1:2],
                                          inv, 0.0, ALU.mult, ALU.add),
           reads=[stat], writes=[stat])
    sc.add("g", lambda e: e.tensor_tensor(stat.ap[:, 3:4], stat.ap[:, 1:2],
                                          stat.ap[:, 3:4], ALU.subtract),
           reads=[stat], writes=[stat])
    sc.add("s", lambda e: e.activation(stat.ap[:, 3:4], stat.ap[:, 3:4], AF.Sqrt,
                                       bias=epsc.ap[:, 0:1], scale=1.0),
           reads=[stat, epsc], writes=[stat])
    sc.add("v", lambda e: e.reciprocal(stat.ap[:, 4:5], stat.ap[:, 3:4]),
           reads=[stat], writes=[stat])


def softplus(sc, xb, spb):
    w = xb.ap.shape[1]
    sp = spb.ap[:, 0:w]
    sc.add("s", lambda e: e.activation(sp, xb.ap, AF.Abs, bias=0.0, scale=1.0),
           reads=[xb], writes=[spb], cost=1050)
    sc.add("s", lambda e: e.activation(sp, sp, AF.Exp, bias=0.0, scale=-1.0),
           reads=[spb], writes=[spb], cost=1050)
    sc.add("s", lambda e: e.activation(sp, sp, AF.Ln, bias=1.0, scale=1.0),
           reads=[spb], writes=[spb], cost=1050)
    sc.add("s", lambda e: e.activation(xb.ap, xb.ap, AF.Relu, bias=0.0, scale=1.0),
           reads=[xb], writes=[xb], cost=1050)
    sc.add("g", lambda e: e.tensor_tensor(xb.ap, xb.ap, sp, ALU.add),
           reads=[xb, spb], writes=[xb], cost=2100)


def build_nc():
    nc = bass.Bass("TRN2", target_bir_lowering=False)
    h = {}
    h["x"] = nc.dram_tensor("x", [SPC, SEQ, DM], F32, kind="ExternalInput").ap()
    for nm, shape in [("wpT", [DM, DM]), ("wfT", [DM, DM]), ("wbT", [DM, DM]),
                      ("wdbcT", [DM, 96]), ("wdtT", [DTR, DM]),
                      ("wgT", [DM, INNER]), ("wuT", [DM, INNER]), ("wdT", [INNER, DM])]:
        h[nm] = nc.dram_tensor(nm, shape, F32, kind="ExternalInput").ap()
    for nm in ("A_log8", "gamma8", "beta8", "bp8", "bf8", "bb8", "dtb8", "D8"):
        cols = NDC * NST if nm == "A_log8" else NDC
        h[nm] = nc.dram_tensor(nm, [P, cols], F32, kind="ExternalInput").ap()
    h["ident"] = nc.dram_tensor("ident", [P, P], F32, kind="ExternalInput").ap()
    h["identb"] = nc.dram_tensor("identb", [P, P], BF16, kind="ExternalInput").ap()
    h["ones"] = nc.dram_tensor("ones", [P, P], F32, kind="ExternalInput").ap()
    h["out"] = nc.dram_tensor("out", [SPC * SEQ, DM], F32, kind="ExternalOutput").ap()
    h["x2scr"] = nc.dram_tensor("x2scr", [SPC * SEQ, DM], F32, kind="Internal").ap()
    h["xq1scr"] = nc.dram_tensor("xq1scr", [DM, SPC * SEQ], BF16, kind="Internal").ap()
    h["xq2scr"] = nc.dram_tensor("xq2scr", [INNER, SPC * SEQ], BF16, kind="Internal").ap()
    h["dscr"] = nc.dram_tensor("dscr", [SPC, 2, DM, SEQ], F32, kind="Internal").ap()
    h["uscr"] = nc.dram_tensor("uscr", [SPC, 2, DM, SEQ], F32, kind="Internal").ap()
    h["bcscr"] = nc.dram_tensor("bcscr", [SPC, 2, 32, SEQ], F32, kind="Internal").ap()
    h["zscr"] = nc.dram_tensor("zscr", [SPC, DM, SEQ], F32, kind="Internal").ap()
    h["udscr"] = nc.dram_tensor("udscr", [SPC, DM, SEQ], F32, kind="Internal").ap()
    h["yscr"] = nc.dram_tensor("yscr", [SPC, DM, SEQ], F32, kind="Internal").ap()
    import os as _os
    if _os.environ.get("K_DBG_OUT"):
        for nm, shape in [("dbg_delta", [DM, SEQ]), ("dbg_du", [DM, SEQ]),
                          ("dbg_y", [DM, SEQ]), ("dbg_ud", [DM, SEQ]),
                          ("dbg_z", [DM, SEQ]), ("dbg_bc", [32, SEQ])]:
            h[nm] = nc.dram_tensor(nm, shape, F32, kind="ExternalOutput").ap()
    x2scrB = Buf(h["x2scr"])
    xq1scrB = Buf(h["xq1scr"])
    xq2scrB = Buf(h["xq2scr"])
    outB = Buf(h["out"])

    sc = Sched()

    with contextlib.ExitStack() as main_stk:
        sems = {ek: main_stk.enter_context(nc.semaphore(f"S{ek}")) for ek in Sched.ENG}
        for j in range(NDSEM):
            sems[f"d{j}"] = main_stk.enter_context(nc.semaphore(f"Sd{j}"))

        def sb(stk, name, shape, dt=F32):
            return Buf(stk.enter_context(nc.sbuf_tensor(name, shape, dt)).ap())

        def psb(stk, name, shape, dt=F32):
            return Buf(stk.enter_context(nc.psum_tensor(name, shape, dt)).ap())

        def dma(dst, src, reads=(), writes=(), xfer=2000.0):
            return sc.add_dma(lambda e, d=dst, s=src: e.dma_start(out=d, in_=s),
                              reads=reads, writes=writes, xfer=xfer)

        ident = sb(main_stk, "identt", [P, P])
        identb = sb(main_stk, "identbt", [P, P], BF16)
        ones = sb(main_stk, "onest", [P, P])
        A_sb = sb(main_stk, "A_sb", [P, NDC * NST])
        D_sb = sb(main_stk, "D_sb", [P, NDC])
        dtb = sb(main_stk, "dtb", [P, NDC])
        bp = sb(main_stk, "bp", [P, NDC])
        bfc = sb(main_stk, "bfc", [P, NDC])
        bbc = sb(main_stk, "bbc", [P, NDC])
        gam = sb(main_stk, "gam", [P, NDC])
        bet = sb(main_stk, "bet", [P, NDC])
        bfold = sb(main_stk, "bfold", [P, NDC])
        ceff = sb(main_stk, "ceff", [P, 2])
        tmpc = sb(main_stk, "tmpc", [P, 2])
        mx1c = sb(main_stk, "mx1c", [P, NTT])
        mx2c = sb(main_stk, "mx2c", [P, NTT])
        stat = sb(main_stk, "stat", [P, 8])
        sc1 = sb(main_stk, "sc1", [P, NTT])
        mgs = sb(main_stk, "mgs", [P, 4])
        colt = sb(main_stk, "colt", [P, 4])
        epsc = sb(main_stk, "epsc", [P, 1])

        def weight_quant(wh, wq, wtmp, wqp, mcol, nkc, width, psw):
            nelem = float(nkc * P * width)
            for kc in range(nkc):
                wt = wtmp[kc % 2]
                dma(wt.ap, wh[kc * P:(kc + 1) * P, :], writes=[wt])
                sc.add("v", lambda e, wt=wt: e.tensor_reduce(
                    colt.ap[:, 0:1], wt.ap, AX.X, ALU.add, apply_absolute_value=True),
                    reads=[wt], writes=[colt], cost=7800)
                if kc == 0:
                    sc.add("g", lambda e: e.tensor_copy(colt.ap[:, 1:2], colt.ap[:, 0:1]),
                           reads=[colt], writes=[colt])
                else:
                    sc.add("g", lambda e: e.tensor_tensor(
                        colt.ap[:, 1:2], colt.ap[:, 1:2], colt.ap[:, 0:1], ALU.add),
                        reads=[colt], writes=[colt])
            sc.add("t", lambda e: e.matmul(psw.ap[:, 0:1], ones.ap, colt.ap[:, 1:2],
                                           start=True, stop=True),
                   reads=[ones, colt], writes=[psw])
            sc.add("s", lambda e: e.copy(colt.ap[:, 2:3], psw.ap[:, 0:1]),
                   reads=[psw], writes=[colt])
            sc.add("g", lambda e: e.tensor_scalar(colt.ap[:, 2:3], colt.ap[:, 2:3],
                                                  1.0 / nelem, 0.0, ALU.mult, ALU.add),
                   reads=[colt], writes=[colt])
            sc.add("g", lambda e: e.tensor_scalar(colt.ap[:, 3:4], colt.ap[:, 2:3],
                                                  3.0e38, EPS, ALU.min, ALU.max),
                   reads=[colt], writes=[colt])
            sc.add("g", lambda e: e.tensor_scalar(
                mgs.ap[:, mcol:mcol + 1], colt.ap[:, 3:4], 1.0 / 127.0, 0.0,
                ALU.mult, ALU.add), reads=[colt], writes=[mgs])
            sc.add("v", lambda e: e.reciprocal(colt.ap[:, 0:1], colt.ap[:, 3:4]),
                   reads=[colt], writes=[colt])
            for kc in range(nkc):
                wt = wtmp[kc % 2]
                dma(wt.ap, wh[kc * P:(kc + 1) * P, :], writes=[wt])
                sc.add("s", lambda e, wt=wt: e.activation(
                    wqp.ap, wt.ap, AF.Identity, bias=0.0, scale=colt.ap[:, 0:1]),
                    reads=[wt, colt], writes=[wqp], cost=3600)
                sc.add("v", lambda e: e.tensor_scalar(
                    wqp.ap, wqp.ap, ceff.ap[:, 0:1], ceff.ap[:, 0:1],
                    ALU.add, ALU.subtract), reads=[wqp, ceff], writes=[wqp], cost=4400)
                sc.add("g", lambda e, kc=kc: e.tensor_scalar(
                    wq.ap[:, kc * width:(kc + 1) * width], wqp.ap, 1.0, -1.0,
                    ALU.min, ALU.max), reads=[wqp], writes=[wq], cost=7800)

        with contextlib.ExitStack() as stk0:
            wpT = sb(stk0, "wpT_s", [P, NDC * DM])
            wfT = sb(stk0, "wfT_s", [P, NDC * DM])
            wbT = sb(stk0, "wbT_s", [P, NDC * DM])
            wdbcT = sb(stk0, "wdbcT_s", [P, NDC * 96])
            wdtT = sb(stk0, "wdtT_s", [DTR, NDC * P])

            for nm, b in [("ident", ident), ("identb", identb), ("ones", ones),
                          ("A_log8", A_sb), ("D8", D_sb), ("dtb8", dtb),
                          ("bp8", bp), ("bf8", bfc), ("bb8", bbc),
                          ("gamma8", gam), ("beta8", bet)]:
                dma(b.ap, h[nm], writes=[b])
            for dc in range(NDC):
                dma(wpT.ap[:, dc * DM:(dc + 1) * DM], h["wpT"][dc * P:(dc + 1) * P, :], writes=[wpT])
                dma(wfT.ap[:, dc * DM:(dc + 1) * DM], h["wfT"][dc * P:(dc + 1) * P, :], writes=[wfT])
                dma(wbT.ap[:, dc * DM:(dc + 1) * DM], h["wbT"][dc * P:(dc + 1) * P, :], writes=[wbT])
                dma(wdbcT.ap[:, dc * 96:(dc + 1) * 96], h["wdbcT"][dc * P:(dc + 1) * P, :], writes=[wdbcT])
                dma(wdtT.ap[:, dc * P:(dc + 1) * P], h["wdtT"][:, dc * P:(dc + 1) * P], writes=[wdtT])

            sc.add("v", lambda e: e.tensor_scalar(tmpc.ap, ones.ap[:, 0:2], 1.0, None,
                                                  ALU.mult), reads=[ones], writes=[tmpc])
            sc.add("g", lambda e: e.tensor_scalar(tmpc.ap, tmpc.ap, 2.0**-40, 1.0,
                                                  ALU.add, ALU.mult),
                   reads=[tmpc], writes=[tmpc])
            sc.add("v", lambda e: e.tensor_scalar(tmpc.ap, tmpc.ap, -1.0, None, ALU.add),
                   reads=[tmpc], writes=[tmpc])
            sc.add("g", lambda e: e.tensor_scalar(ceff.ap, tmpc.ap, KMUL, C32,
                                                  ALU.mult, ALU.add),
                   reads=[tmpc], writes=[ceff])

            sc.add("g", lambda e: e.tensor_scalar(epsc.ap, ones.ap[:, 0:1], EPS, 0.0,
                                                  ALU.mult, ALU.add),
                   reads=[ones], writes=[epsc])
            sc.add("s", lambda e: e.activation(A_sb.ap, A_sb.ap, AF.Exp,
                                               bias=0.0, scale=1.0),
                   reads=[A_sb], writes=[A_sb])
            sc.add("g", lambda e: e.tensor_scalar(A_sb.ap, A_sb.ap, -1.0, 0.0,
                                                  ALU.mult, ALU.add),
                   reads=[A_sb], writes=[A_sb])

            with contextlib.ExitStack() as stkA:
                TSC = 512
                NQ = SEQ // TSC
                CPQ = TSC // TA
                x_sbP = [sb(stkA, f"x_sb{i}", [P, DM]) for i in range(2)]
                xn_sb = sb(stkA, "xn_sb", [P, DM])
                xpTP = [sb(stkA, f"xpT{i}", [P, NDC * TA]) for i in range(2)]
                uT = sb(stkA, "uT", [P, NDC * TA])
                deltaT1 = sb(stkA, "deltaT1", [P, NDC * TA])
                duT1 = sb(stkA, "duT1", [P, NDC * TA])
                spP = [sb(stkA, "sp0", [P, DM])]
                dbcP = [Buf(stkA.enter_context(nc.sbuf_tensor(f"dbc{i}", [96, TA], F32)).ap())
                        for i in range(2)]
                statP = [sb(stkA, f"stat{i}", [P, 8]) for i in range(2)]
                TSC2 = 256
                NF = 2
                NG = NST // NF
                SW = NF * (TSC2 + 1)
                dAll = sb(stkA, "dAll", [P, NDC * TSC2])
                uAll = sb(stkA, "uAll", [P, NDC * TSC2])
                dAllB = [Buf(None) for _ in range(4)]
                uAllB = [Buf(None) for _ in range(4)]
                brepP = [sb(stkA, f"brep{i}", [P, NF * TSC2]) for i in range(2)]
                crepP = [sb(stkA, f"crep{i}", [P, NF * TSC2]) for i in range(2)]
                scnA = [sb(stkA, f"scnA{i}", [P, SW]) for i in range(2)]
                scnB = [sb(stkA, f"scnB{i}", [P, SW]) for i in range(2)]
                scnH = [sb(stkA, f"scnH{i}", [P, SW]) for i in range(2)]
                ysA = sb(stkA, "ysA", [P, NDC * TSC2])
                yp0 = psb(stkA, "yp0", [P, 2 * TSC2])
                ypP = [Buf(yp0.ap[:, 0:TSC2]), Buf(yp0.ap[:, TSC2:2 * TSC2])]
                hcarB = [sb(stkA, f"hcar{i}", [P, P]) for i in range(2)]
                hcarR = [[[Buf(None) for _ in range(NG)] for _ in range(NDC)]
                         for _ in range(2)]
                ysAB = [Buf(None) for _ in range(NDC)]
                ya = sb(stkA, "ya", [P, DM])
                uda = sb(stkA, "uda", [P, DM])
                xa2 = sb(stkA, "xa2", [P, DM])
                spa = sb(stkA, "spa", [P, DM])
                stata = sb(stkA, "stata", [P, 8])
                xq1b = sb(stkA, "xq1b", [P, DM], BF16)
                xq1T = sb(stkA, "xq1T", [P, NDC * TA], BF16)
                psA = psb(stkA, "psA", [P, 512])
                psA2 = psb(stkA, "psA2", [P, 512])
                psT0 = psb(stkA, "psT0", [P, 512])
                psT1 = psb(stkA, "psT1", [P, 512])
                psDB = psb(stkA, "psDB", [96, TA])
                psTB0 = psb(stkA, "psTB0", [P, 512], BF16)
                psTB1 = psb(stkA, "psTB1", [P, 512], BF16)
                psmm = [psA, psA2]

                dscrB = [[[Buf(None) for _ in range(NCH)] for _ in range(2)] for _ in range(SPC)]
                uscrB = [[[Buf(None) for _ in range(NCH)] for _ in range(2)] for _ in range(SPC)]
                bcscrB = [[[Buf(None) for _ in range(NCH)] for _ in range(2)] for _ in range(SPC)]
                zscrB = [[Buf(None) for _ in range(NCH)] for _ in range(SPC)]
                udscrB = [[Buf(None) for _ in range(NCH)] for _ in range(SPC)]
                yscrB = [[[Buf(None) for _ in range(4)] for _ in range(SEQ // TSC2)]
                         for _ in range(SPC)]

                for a_ in scnA:
                    a3m = a_.ap.rearrange("p (nf t) -> p nf t", nf=NF)
                    sc.add("g", lambda e, a3m=a3m: e.memset(a3m[:, :, 0:1], 0.0),
                           writes=[a_])

                for oc in range(NDC):
                    for dc in range(NDC):
                        sc.add("t", lambda e, oc=oc, dc=dc: e.matmul(
                            psA.ap[:, 0:1],
                            wpT.ap[:, dc * DM + oc * P: dc * DM + (oc + 1) * P],
                            bet.ap[:, dc:dc + 1],
                            start=(dc == 0), stop=(dc == NDC - 1)),
                            reads=[wpT, bet], writes=[psA], inc=(dc == NDC - 1))
                    sc.add("s", lambda e, oc=oc: e.activation(
                        bfold.ap[:, oc:oc + 1], psA.ap[:, 0:1], AF.Identity,
                        bias=bp.ap[:, oc:oc + 1], scale=1.0),
                        reads=[psA, bp], writes=[bfold])
                for dc in range(NDC):
                    sc.add("v", lambda e, dc=dc: e.tensor_scalar(
                        wpT.ap[:, dc * DM:(dc + 1) * DM], wpT.ap[:, dc * DM:(dc + 1) * DM],
                        gam.ap[:, dc:dc + 1], 1.0, ALU.mult, ALU.mult),
                        reads=[wpT, gam], writes=[wpT])

                def fm3(ap2d):
                    return ap2d.rearrange("p (dc t) -> p dc t", dc=NDC)

                def do_a1(s, c, ti, x_sb, xpT, deltaT, duT, udc, sp, dbc, stat):
                    t0 = c * TA
                    dma(x_sb.ap, h["x"][s, t0:t0 + TA, :], writes=[x_sb])
                    sc.add("s", lambda e: e.activation(
                        x_sb.ap, x_sb.ap, AF.Identity, bias=0.0, scale=1.0,
                        accum_out=stat.ap[:, 0:1]), reads=[x_sb], writes=[x_sb, stat], cost=1100)
                    sc.add("s", lambda e: e.activation(
                        sp.ap, x_sb.ap, AF.Square, bias=0.0, scale=1.0,
                        accum_out=stat.ap[:, 1:2]), reads=[x_sb], writes=[sp, stat], cost=1100)
                    ln_stats(sc, stat, DM, epsc)
                    sc.add("g", lambda e: e.tensor_scalar(
                        xn_sb.ap, x_sb.ap, stat.ap[:, 2:3], stat.ap[:, 4:5],
                        ALU.add, ALU.mult), reads=[x_sb, stat], writes=[xn_sb], cost=1100)
                    for half in range(2):
                        pst = [psT0, psT1][half]
                        for j in range(4):
                            dc = half * 4 + j
                            sc.add("t", lambda e, dc=dc, j=j, pst=pst: e.transpose(
                                pst.ap[:, j * P:(j + 1) * P],
                                xn_sb.ap[:, dc * P:(dc + 1) * P], ident.ap),
                                reads=[xn_sb, ident], writes=[pst], inc=(j == 3))
                        sc.add("s", lambda e, half=half, pst=pst: e.copy(
                            sp.ap[:, half * 512:(half + 1) * 512], pst.ap),
                            reads=[pst], writes=[sp], cost=600)
                    for oc in range(NDC):
                        pm = psmm[oc % 2]
                        for dc in range(NDC):
                            sc.add("t", lambda e, oc=oc, dc=dc, pm=pm: e.matmul(
                                pm.ap[:, 0:TA],
                                wpT.ap[:, dc * DM + oc * P: dc * DM + (oc + 1) * P],
                                sp.ap[:, dc * P:(dc + 1) * P],
                                start=(dc == 0), stop=(dc == NDC - 1)),
                                reads=[wpT, sp], writes=[pm], inc=(dc == NDC - 1))
                        sc.add("s", lambda e, oc=oc, pm=pm: e.activation(
                            xpT.ap[:, oc * TA:(oc + 1) * TA], pm.ap[:, 0:TA],
                            AF.Identity, bias=bfold.ap[:, oc:oc + 1], scale=1.0),
                            reads=[pm, bfold], writes=[xpT], cost=300)
                    for br, (wb, bb_) in enumerate([(wfT, bfc), (wbT, bbc)]):
                        for oc in range(NDC):
                            pm = psmm[oc % 2]
                            for dc in range(NDC):
                                sc.add("t", lambda e, oc=oc, dc=dc, pm=pm, wb=wb: e.matmul(
                                    pm.ap[:, 0:TA],
                                    wb.ap[:, dc * DM + oc * P: dc * DM + (oc + 1) * P],
                                    xpT.ap[:, dc * TA:(dc + 1) * TA],
                                    start=(dc == 0), stop=(dc == NDC - 1)),
                                    reads=[wb, xpT], writes=[pm], inc=(dc == NDC - 1))
                            sc.add("s", lambda e, oc=oc, pm=pm, bb_=bb_: e.activation(
                                uT.ap[:, oc * TA:(oc + 1) * TA], pm.ap[:, 0:TA],
                                AF.Identity, bias=bb_.ap[:, oc:oc + 1], scale=1.0),
                                reads=[pm, bb_], writes=[uT], cost=300)
                        softplus(sc, uT, sp)
                        for dc in range(NDC):
                            sc.add("t", lambda e, dc=dc: e.matmul(
                                psDB.ap,
                                wdbcT.ap[:, dc * 96:(dc + 1) * 96],
                                uT.ap[:, dc * TA:(dc + 1) * TA],
                                start=(dc == 0), stop=(dc == NDC - 1)),
                                reads=[wdbcT, uT], writes=[psDB], inc=(dc == NDC - 1))
                        sc.add("s", lambda e: e.copy(dbc.ap, psDB.ap),
                               reads=[psDB], writes=[dbc], cost=300)
                        dma(h["bcscr"][s, br][:, t0:t0 + TA], dbc.ap[DTR:96, :],
                            reads=[dbc], writes=[bcscrB[s][br][c]], xfer=800)
                        for g4 in range(2):
                            pst = [psT0, psT1][g4]
                            for j in range(4):
                                dc = g4 * 4 + j
                                sc.add("t", lambda e, dc=dc, j=j, pst=pst: e.matmul(
                                    pst.ap[:, j * TA:(j + 1) * TA],
                                    wdtT.ap[:, dc * P:(dc + 1) * P],
                                    dbc.ap[0:DTR, :],
                                    start=True, stop=True),
                                    reads=[wdtT, dbc], writes=[pst], inc=(j == 3))
                            for j in range(4):
                                dc = g4 * 4 + j
                                sc.add("s", lambda e, dc=dc, j=j, pst=pst: e.activation(
                                    deltaT.ap[:, dc * TA:(dc + 1) * TA],
                                    pst.ap[:, j * TA:(j + 1) * TA],
                                    AF.Identity, bias=dtb.ap[:, dc:dc + 1], scale=1.0),
                                    reads=[pst, dtb], writes=[deltaT], cost=300)
                        softplus(sc, deltaT, sp)
                        sc.add("v", lambda e: e.tensor_tensor(
                            duT.ap, deltaT.ap, uT.ap, ALU.mult),
                            reads=[deltaT, uT], writes=[duT], cost=1200)
                        dma(h["dscr"][s, br].rearrange("(dc p) t -> p dc t", p=P)
                            [:, :, t0:t0 + TA], fm3(deltaT.ap),
                            reads=[deltaT], writes=[dscrB[s][br][c]], xfer=2000)
                        dma(h["uscr"][s, br].rearrange("(dc p) t -> p dc t", p=P)
                            [:, :, t0:t0 + TA], fm3(duT.ap),
                            reads=[duT], writes=[uscrB[s][br][c]], xfer=2000)
                        if br == 0:
                            for dc in range(NDC):
                                sc.add("g", lambda e, dc=dc: e.tensor_scalar(
                                    udc.ap[:, dc * TA:(dc + 1) * TA],
                                    uT.ap[:, dc * TA:(dc + 1) * TA],
                                    D_sb.ap[:, dc:dc + 1], 0.0, ALU.mult, ALU.add),
                                    reads=[uT, D_sb], writes=[udc], cost=300)
                        else:
                            for dc in range(NDC):
                                sc.add("v", lambda e, dc=dc: e.scalar_tensor_tensor(
                                    udc.ap[:, dc * TA:(dc + 1) * TA],
                                    uT.ap[:, dc * TA:(dc + 1) * TA],
                                    D_sb.ap[:, dc:dc + 1],
                                    udc.ap[:, dc * TA:(dc + 1) * TA],
                                    ALU.mult, ALU.add),
                                    reads=[uT, D_sb, udc], writes=[udc], cost=300)
                    sc.add("s", lambda e: e.activation(xpT.ap, xpT.ap, AF.Silu,
                                                       bias=0.0, scale=1.0),
                           reads=[xpT], writes=[xpT], cost=1100)
                    dma(h["zscr"][s].rearrange("(dc p) t -> p dc t", p=P)
                        [:, :, t0:t0 + TA], fm3(xpT.ap),
                        reads=[xpT], writes=[zscrB[s][c]], xfer=2000)
                    dma(h["udscr"][s].rearrange("(dc p) t -> p dc t", p=P)
                        [:, :, t0:t0 + TA], fm3(udc.ap),
                        reads=[udc], writes=[udscrB[s][c]], xfer=2000)

                scnt = [0]

                def do_scan(s, q):
                    for qq in (2 * q, 2 * q + 1):
                        t0 = qq * TSC2
                        cdep = [2 * qq, 2 * qq + 1]
                        for br in range(2):
                            ddep = [dscrB[s][br][c_] for c_ in cdep]
                            udep = [uscrB[s][br][c_] for c_ in cdep]
                            bdep = [bcscrB[s][br][c_] for c_ in cdep]
                            for j in range(4):
                                dma(dAll.ap.rearrange("p (dc t) -> p dc t", dc=NDC)
                                    [:, 2 * j:2 * j + 2, :],
                                    h["dscr"][s, br].rearrange("(dc p) t -> p dc t", p=P)
                                    [:, 2 * j:2 * j + 2, t0:t0 + TSC2],
                                    reads=ddep, writes=[dAllB[j]], xfer=10000)
                                dma(uAll.ap.rearrange("p (dc t) -> p dc t", dc=NDC)
                                    [:, 2 * j:2 * j + 2, :],
                                    h["uscr"][s, br].rearrange("(dc p) t -> p dc t", p=P)
                                    [:, 2 * j:2 * j + 2, t0:t0 + TSC2],
                                    reads=udep, writes=[uAllB[j]], xfer=10000)
                            for ng in range(NG):
                                kb = (br * NG + ng) % 2
                                brep, crep = brepP[kb], crepP[kb]
                                dma(brep.ap.rearrange("p (nf t) -> p nf t", nf=NF),
                                    h["bcscr"][s, br][NF * ng:NF * ng + NF, t0:t0 + TSC2]
                                    .unsqueeze(0).broadcast_to([P, NF, TSC2]),
                                    reads=bdep, writes=[brep], xfer=5500)
                                dma(crep.ap.rearrange("p (nf t) -> p nf t", nf=NF),
                                    h["bcscr"][s, br][NST + NF * ng:NST + NF * ng + NF,
                                                      t0:t0 + TSC2]
                                    .unsqueeze(0).broadcast_to([P, NF, TSC2]),
                                    reads=bdep, writes=[crep], xfer=5500)
                                for i in range(NDC):
                                    k = scnt[0] % 2
                                    scnt[0] += 1
                                    a_, b_, h_ = scnA[k], scnB[k], scnH[k]
                                    a3 = a_.ap.rearrange("p (nf t) -> p nf t", nf=NF)
                                    b3 = b_.ap.rearrange("p (nf t) -> p nf t", nf=NF)
                                    h3 = h_.ap.rearrange("p (nf t) -> p nf t", nf=NF)
                                    for nf in range(NF):
                                        col = i * NST + ng * NF + nf
                                        o0 = nf * (TSC2 + 1) + 1
                                        sc.add("s", lambda e, a_=a_, o0=o0, i=i, col=col:
                                               e.activation(
                                                   a_.ap[:, o0:o0 + TSC2],
                                                   dAll.ap[:, i * TSC2:(i + 1) * TSC2],
                                                   AF.Exp, bias=0.0,
                                                   scale=A_sb.ap[:, col:col + 1]),
                                               reads=[dAllB[i // 2], A_sb],
                                               writes=[a_], cost=500)
                                    cc = i * NST + ng * NF
                                    if qq == 0:
                                        sc.add("g", lambda e, b3=b3: e.memset(
                                            b3[:, :, 0:1], 0.0),
                                            reads=[], writes=[b_], cost=100)
                                    else:
                                        sc.add("s", lambda e, b3=b3, br=br, cc=cc: e.copy(
                                            b3[:, :, 0:1].rearrange("p nf o -> p (nf o)"),
                                            hcarB[br].ap[:, cc:cc + NF]),
                                            reads=[hcarR[br][i][ng]], writes=[b_], cost=230)
                                    sc.add("g", lambda e, b3=b3, i=i, kb=kb: e.tensor_tensor(
                                        b3[:, :, 1:TSC2 + 1],
                                        uAll.ap[:, i * TSC2:(i + 1) * TSC2]
                                        .unsqueeze(1).broadcast_to([P, NF, TSC2]),
                                        brepP[kb].ap.rearrange("p (nf t) -> p nf t", nf=NF),
                                        ALU.mult),
                                        reads=[uAllB[i // 2], brepP[kb]], writes=[b_],
                                        cost=1150)
                                    sc.add("v", lambda e, h_=h_, a_=a_, b_=b_:
                                           e.tensor_tensor_scan(
                                               h_.ap, a_.ap, b_.ap, 0.0, ALU.mult, ALU.add),
                                           reads=[a_, b_], writes=[h_], cost=1120)
                                    sc.add("s", lambda e, h3=h3, br=br, cc=cc: e.copy(
                                        hcarB[br].ap[:, cc:cc + NF],
                                        h3[:, :, TSC2:TSC2 + 1].rearrange("p nf o -> p (nf o)")),
                                        reads=[h_], writes=[hcarR[br][i][ng]], cost=230)
                                    sc.add("g", lambda e, b_=b_, h3=h3, kb=kb: e.tensor_tensor(
                                        b_.ap[:, 0:NF * TSC2]
                                        .rearrange("p (nf t) -> p nf t", nf=NF),
                                        h3[:, :, 1:TSC2 + 1],
                                        crepP[kb].ap.rearrange("p (nf t) -> p nf t", nf=NF),
                                        ALU.mult),
                                        reads=[h_, crepP[kb]], writes=[b_], cost=1150)
                                    hcv = b_.ap[:, 0:NF * TSC2].rearrange(
                                        "p (nf t) -> p nf t", nf=NF).transpose([0, 2, 1])
                                    if br == 0 and ng == 0:
                                        sc.add("v", lambda e, hcv=hcv, i=i: e.tensor_reduce(
                                            ysA.ap[:, i * TSC2:(i + 1) * TSC2],
                                            hcv, AX.X, ALU.add),
                                            reads=[b_], writes=[ysAB[i]], cost=1020)
                                    else:
                                        yp = ypP[k]
                                        sc.add("v", lambda e, hcv=hcv, yp=yp: e.tensor_reduce(
                                            yp.ap, hcv, AX.X, ALU.add),
                                            reads=[b_], writes=[yp], cost=1020)
                                        sc.add("v", lambda e, yp=yp, i=i:
                                               e.scalar_tensor_tensor(
                                                   ysA.ap[:, i * TSC2:(i + 1) * TSC2],
                                                   yp.ap, 1.0,
                                                   ysA.ap[:, i * TSC2:(i + 1) * TSC2],
                                                   ALU.mult, ALU.add),
                                               reads=[yp, ysAB[i]], writes=[ysAB[i]], cost=420)
                        for j in range(4):
                            dma(h["yscr"][s].rearrange("(dc p) t -> p dc t", p=P)
                                [:, 2 * j:2 * j + 2, t0:t0 + TSC2],
                                ysA.ap.rearrange("p (dc t) -> p dc t", dc=NDC)
                                [:, 2 * j:2 * j + 2, :],
                                reads=[ysAB[2 * j], ysAB[2 * j + 1]],
                                writes=[yscrB[s][qq][j]], xfer=10000)

                def do_a2(s, c, ti):
                    t0 = c * TA
                    q = c // CPQ
                    dma(ya.ap, h["yscr"][s].rearrange("(dc p) t -> p dc t", p=P)
                        [:, :, t0:t0 + TA], reads=list(yscrB[s][c // 2]), writes=[ya],
                        xfer=2000)
                    dma(uda.ap, h["udscr"][s].rearrange("(dc p) t -> p dc t", p=P)
                        [:, :, t0:t0 + TA], reads=[udscrB[s][c]], writes=[uda], xfer=2000)
                    dma(spa.ap, h["zscr"][s].rearrange("(dc p) t -> p dc t", p=P)
                        [:, :, t0:t0 + TA], reads=[zscrB[s][c]], writes=[spa], xfer=2000)
                    dma(xa2.ap, h["x"][s, t0:t0 + TA, :], writes=[xa2], xfer=2000)
                    sc.add("v", lambda e: e.tensor_tensor(ya.ap, ya.ap, uda.ap, ALU.add),
                           reads=[ya, uda], writes=[ya], cost=1200)
                    sc.add("g", lambda e: e.tensor_tensor(ya.ap, ya.ap, spa.ap, ALU.mult),
                           reads=[ya, spa], writes=[ya], cost=2200)
                    for half in range(2):
                        pst = [psT0, psT1][half]
                        for j in range(4):
                            dc = half * 4 + j
                            sc.add("t", lambda e, dc=dc, j=j, pst=pst: e.transpose(
                                pst.ap[:, j * P:(j + 1) * P],
                                ya.ap[:, dc * TA:(dc + 1) * TA], ident.ap),
                                reads=[ya, ident], writes=[pst], inc=(j == 3))
                        sc.add("v", lambda e, half=half, pst=pst: e.scalar_tensor_tensor(
                            xa2.ap[:, half * 512:(half + 1) * 512],
                            xa2.ap[:, half * 512:(half + 1) * 512], 2.0, pst.ap,
                            ALU.mult, ALU.add), reads=[xa2, pst], writes=[xa2], cost=700)
                    sc.add("s", lambda e: e.activation(
                        xa2.ap, xa2.ap, AF.Identity, bias=0.0, scale=1.0,
                        accum_out=stata.ap[:, 0:1]), reads=[xa2], writes=[xa2, stata], cost=1100)
                    sc.add("s", lambda e: e.activation(
                        spa.ap, xa2.ap, AF.Square, bias=0.0, scale=1.0,
                        accum_out=stata.ap[:, 1:2]), reads=[xa2], writes=[spa, stata], cost=1100)
                    ln_stats(sc, stata, DM, epsc)
                    sc.add("g", lambda e: e.tensor_scalar(
                        xa2.ap, xa2.ap, stata.ap[:, 2:3], stata.ap[:, 4:5],
                        ALU.add, ALU.mult), reads=[xa2, stata], writes=[xa2], cost=1100)
                    dma(h["x2scr"][ti * TA:(ti + 1) * TA, :], xa2.ap,
                        reads=[xa2], writes=[x2scrB], xfer=2000)
                    sc.add("v", lambda e: e.tensor_reduce(
                        stata.ap[:, 5:6], xa2.ap, AX.X, ALU.max,
                        apply_absolute_value=True), reads=[xa2], writes=[stata], cost=2000)
                    sc.add("g", lambda e, ti=ti: e.tensor_scalar(
                        mx1c.ap[:, ti:ti + 1], stata.ap[:, 5:6], 3.0e38, EPS,
                        ALU.min, ALU.max), reads=[stata], writes=[mx1c], cost=300)
                    sc.add("v", lambda e, ti=ti: e.reciprocal(
                        stata.ap[:, 6:7], mx1c.ap[:, ti:ti + 1]),
                        reads=[mx1c], writes=[stata], cost=300)
                    sc.add("g", lambda e, ti=ti: e.tensor_scalar(
                        sc1.ap[:, ti:ti + 1], stata.ap[:, 6:7], 127.0, 0.0,
                        ALU.mult, ALU.add), reads=[stata], writes=[sc1], cost=300)
                    sc.add("s", lambda e, ti=ti: e.activation(
                        spa.ap, xa2.ap, AF.Identity, bias=0.0,
                        scale=sc1.ap[:, ti:ti + 1]), reads=[xa2, sc1], writes=[spa], cost=1100)
                    sc.add("v", lambda e: e.tensor_scalar(
                        spa.ap, spa.ap, ceff.ap[:, 0:1], ceff.ap[:, 0:1],
                        ALU.add, ALU.subtract), reads=[spa, ceff], writes=[spa], cost=1200)
                    sc.add("g", lambda e: e.tensor_scalar(
                        xq1b.ap, spa.ap, 127.0, -128.0, ALU.min, ALU.max),
                        reads=[spa], writes=[xq1b], cost=2000)
                    for half in range(2):
                        pst = [psTB0, psTB1][half]
                        for j in range(4):
                            dc = half * 4 + j
                            sc.add("t", lambda e, dc=dc, j=j, pst=pst: e.transpose(
                                pst.ap[:, j * P:(j + 1) * P],
                                xq1b.ap[:, dc * P:(dc + 1) * P], identb.ap),
                                reads=[xq1b, identb], writes=[pst], inc=(j == 3))
                        sc.add("s", lambda e, half=half, pst=pst: e.copy(
                            xq1T.ap[:, half * 512:(half + 1) * 512], pst.ap),
                            reads=[pst], writes=[xq1T], cost=600)
                    dma(h["xq1scr"].rearrange("(dc p) t -> p dc t", p=P)
                        [:, :, ti * TA:(ti + 1) * TA],
                        xq1T.ap.rearrange("p (dc t) -> p dc t", dc=NDC),
                        reads=[xq1T], writes=[xq1scrB], xfer=1200)

                for s in range(SPC):
                    for c in range(NCH):
                        pi = c % 2
                        do_a1(s, c, s * NCH + c, x_sbP[pi], xpTP[pi], deltaT1,
                              duT1, x_sbP[pi], spP[0], dbcP[pi], statP[pi])
                        if c % CPQ == CPQ - 1:
                            do_scan(s, c // CPQ)
                    for c in range(NCH):
                        do_a2(s, c, s * NCH + c)

            import os as _os
            if _os.environ.get("K_DBG_OUT"):
                dbgB = Buf(None)
                dma(h["dbg_delta"], h["dscr"][0, 0], reads=list(dscrB[0][0]), writes=[dbgB], xfer=30000)
                dma(h["dbg_du"], h["uscr"][0, 0], reads=list(uscrB[0][0]), writes=[dbgB], xfer=30000)
                dma(h["dbg_y"], h["yscr"][0], reads=[b for q_ in yscrB[0] for b in q_], writes=[dbgB], xfer=30000)
                dma(h["dbg_ud"], h["udscr"][0], reads=list(udscrB[0]), writes=[dbgB], xfer=30000)
                dma(h["dbg_z"], h["zscr"][0], reads=list(zscrB[0]), writes=[dbgB], xfer=30000)
                dma(h["dbg_bc"], h["bcscr"][0, 0], reads=list(bcscrB[0][0]), writes=[dbgB], xfer=30000)
            sc.barrier()

        with contextlib.ExitStack() as stkB1:
            wgq = sb(stkB1, "wgq", [P, NDC * INNER], BF16)
            wuq = sb(stkB1, "wuq", [P, NDC * INNER], BF16)
            with contextlib.ExitStack() as stkQ:
                wtmp = [sb(stkQ, f"wtmp{i}", [P, INNER]) for i in range(2)]
                wqp = sb(stkQ, "wqp", [P, INNER])
                psW = psb(stkQ, "psW", [P, 8])
                weight_quant(h["wgT"], wgq, wtmp, wqp, 0, NDC, INNER, psW)
                weight_quant(h["wuT"], wuq, wtmp, wqp, 1, NDC, INNER, psW)
                sc.barrier()

            xq1c = [sb(stkB1, f"xq1c{i}", [P, NDC * TA], BF16) for i in range(2)]
            gate = sb(stkB1, "gate", [P, INNER])
            prod = sb(stkB1, "prod", [P, INNER])
            xq2b = sb(stkB1, "xq2b", [P, INNER], BF16)
            xq2T = sb(stkB1, "xq2T", [P, NIC * TA], BF16)
            psG = psb(stkB1, "psG", [P, 512])
            psU = psb(stkB1, "psU", [P, 512])
            psT0b = psb(stkB1, "psT0b", [P, 512], BF16)
            psT1b = psb(stkB1, "psT1b", [P, 512], BF16)
            sgc = sb(stkB1, "sgc", [P, 2])

            for ti in range(NTT):
                xc = xq1c[ti % 2]
                dma(xc.ap.rearrange("p (dc t) -> p dc t", dc=NDC),
                    h["xq1scr"].rearrange("(dc p) t -> p dc t", p=P)
                    [:, :, ti * TA:(ti + 1) * TA],
                    reads=[xq1scrB], writes=[xc])
                sc.add("g", lambda e, ti=ti: e.tensor_tensor(
                    sgc.ap[:, 0:1], mx1c.ap[:, ti:ti + 1], mgs.ap[:, 0:1], ALU.mult),
                    reads=[mx1c, mgs], writes=[sgc])
                sc.add("g", lambda e, ti=ti: e.tensor_tensor(
                    sgc.ap[:, 1:2], mx1c.ap[:, ti:ti + 1], mgs.ap[:, 1:2], ALU.mult),
                    reads=[mx1c, mgs], writes=[sgc])
                for oc in range(8):
                    for dc in range(NDC):
                        sc.add("t", lambda e, oc=oc, dc=dc, xc=xc: e.matmul(
                            psG.ap, xc.ap[:, dc * TA:(dc + 1) * TA],
                            wgq.ap[:, dc * INNER + oc * 512:dc * INNER + (oc + 1) * 512],
                            start=(dc == 0), stop=(dc == NDC - 1)),
                            reads=[xc, wgq], writes=[psG], inc=(dc == NDC - 1))
                    sc.add("s", lambda e, oc=oc: e.activation(
                        gate.ap[:, oc * 512:(oc + 1) * 512], psG.ap, AF.Silu,
                        bias=0.0, scale=sgc.ap[:, 0:1]),
                        reads=[psG, sgc], writes=[gate], cost=600)
                    for dc in range(NDC):
                        sc.add("t", lambda e, oc=oc, dc=dc, xc=xc: e.matmul(
                            psU.ap, xc.ap[:, dc * TA:(dc + 1) * TA],
                            wuq.ap[:, dc * INNER + oc * 512:dc * INNER + (oc + 1) * 512],
                            start=(dc == 0), stop=(dc == NDC - 1)),
                            reads=[xc, wuq], writes=[psU], inc=(dc == NDC - 1))
                    sc.add("v", lambda e, oc=oc: e.scalar_tensor_tensor(
                        prod.ap[:, oc * 512:(oc + 1) * 512], psU.ap, sgc.ap[:, 1:2],
                        gate.ap[:, oc * 512:(oc + 1) * 512], ALU.mult, ALU.mult),
                        reads=[psU, sgc, gate], writes=[prod], cost=700)
                sc.add("v", lambda e: e.tensor_reduce(
                    stat.ap[:, 5:6], prod.ap, AX.X, ALU.max,
                    apply_absolute_value=True), reads=[prod], writes=[stat], cost=7800)
                sc.add("g", lambda e, ti=ti: e.tensor_scalar(
                    mx2c.ap[:, ti:ti + 1], stat.ap[:, 5:6], 3.0e38, EPS,
                    ALU.min, ALU.max), reads=[stat], writes=[mx2c])
                sc.add("v", lambda e, ti=ti: e.reciprocal(
                    stat.ap[:, 6:7], mx2c.ap[:, ti:ti + 1]), reads=[mx2c], writes=[stat])
                sc.add("g", lambda e: e.tensor_scalar(
                    stat.ap[:, 7:8], stat.ap[:, 6:7], 127.0, 0.0, ALU.mult, ALU.add),
                    reads=[stat], writes=[stat])
                sc.add("s", lambda e: e.activation(
                    gate.ap, prod.ap, AF.Identity, bias=0.0, scale=stat.ap[:, 7:8]),
                    reads=[prod, stat], writes=[gate], cost=3600)
                sc.add("v", lambda e: e.tensor_scalar(
                    gate.ap, gate.ap, ceff.ap[:, 0:1], ceff.ap[:, 0:1],
                    ALU.add, ALU.subtract), reads=[gate, ceff], writes=[gate], cost=4400)
                sc.add("g", lambda e: e.tensor_scalar(
                    xq2b.ap, gate.ap, 127.0, -128.0, ALU.min, ALU.max),
                    reads=[gate], writes=[xq2b], cost=7800)
                for q8 in range(8):
                    pst = [psT0b, psT1b][q8 % 2]
                    for j in range(4):
                        ocb = q8 * 4 + j
                        sc.add("t", lambda e, ocb=ocb, j=j, pst=pst: e.transpose(
                            pst.ap[:, j * P:(j + 1) * P],
                            xq2b.ap[:, ocb * P:(ocb + 1) * P], identb.ap),
                            reads=[xq2b, identb], writes=[pst], inc=(j == 3))
                    sc.add("s", lambda e, q8=q8, pst=pst: e.copy(
                        xq2T.ap[:, q8 * 512:(q8 + 1) * 512], pst.ap),
                        reads=[pst], writes=[xq2T])
                dma(h["xq2scr"].rearrange("(kc p) t -> p kc t", p=P)
                    [:, :, ti * TA:(ti + 1) * TA],
                    xq2T.ap.rearrange("p (kc t) -> p kc t", kc=NIC),
                    reads=[xq2T], writes=[xq2scrB])
            sc.barrier()

        with contextlib.ExitStack() as stkB2:
            wdq = sb(stkB2, "wdq", [P, NIC * DM], BF16)
            with contextlib.ExitStack() as stkQ2:
                wtmp2 = [sb(stkQ2, f"wtmp2_{i}", [P, DM]) for i in range(2)]
                wqp2 = sb(stkQ2, "wqp2", [P, DM])
                psW2 = psb(stkQ2, "psW2", [P, 8])
                weight_quant(h["wdT"], wdq, wtmp2, wqp2, 2, NIC, DM, psW2)
                sc.barrier()

            xq2c = [sb(stkB2, f"xq2c{i}", [P, NIC * TA], BF16) for i in range(2)]
            x2c = [sb(stkB2, f"x2c{i}", [P, DM]) for i in range(2)]
            outb = sb(stkB2, "outb", [P, DM])
            psD0 = psb(stkB2, "psD0", [P, 512])
            psD1 = psb(stkB2, "psD1", [P, 512])
            sdc = sb(stkB2, "sdc", [P, 1])

            for ti in range(NTT):
                xc = xq2c[ti % 2]
                x2 = x2c[ti % 2]
                dma(xc.ap.rearrange("p (kc t) -> p kc t", kc=NIC),
                    h["xq2scr"].rearrange("(kc p) t -> p kc t", p=P)
                    [:, :, ti * TA:(ti + 1) * TA],
                    reads=[xq2scrB], writes=[xc])
                dma(x2.ap, h["x2scr"][ti * TA:(ti + 1) * TA, :],
                    reads=[x2scrB], writes=[x2])
                sc.add("g", lambda e, ti=ti: e.tensor_tensor(
                    sdc.ap, mx2c.ap[:, ti:ti + 1], mgs.ap[:, 2:3], ALU.mult),
                    reads=[mx2c, mgs], writes=[sdc])
                for oc2 in range(2):
                    pm = [psD0, psD1][oc2]
                    for kc in range(NIC):
                        sc.add("t", lambda e, oc2=oc2, kc=kc, xc=xc, pm=pm: e.matmul(
                            pm.ap, xc.ap[:, kc * TA:(kc + 1) * TA],
                            wdq.ap[:, kc * DM + oc2 * 512:kc * DM + (oc2 + 1) * 512],
                            start=(kc == 0), stop=(kc == NIC - 1)),
                            reads=[xc, wdq], writes=[pm], inc=(kc == NIC - 1))
                    sc.add("v", lambda e, oc2=oc2, x2=x2, pm=pm: e.scalar_tensor_tensor(
                        x2.ap[:, oc2 * 512:(oc2 + 1) * 512], pm.ap, sdc.ap[:, 0:1],
                        x2.ap[:, oc2 * 512:(oc2 + 1) * 512], ALU.mult, ALU.add),
                        reads=[pm, sdc, x2], writes=[x2])
                sc.add("s", lambda e, x2=x2: e.activation(
                    outb.ap, x2.ap, AF.Identity, bias=0.0, scale=1.0,
                    accum_out=stat.ap[:, 0:1]), reads=[x2], writes=[outb, stat], cost=1100)
                sc.add("s", lambda e, x2=x2: e.activation(
                    outb.ap, x2.ap, AF.Square, bias=0.0, scale=1.0,
                    accum_out=stat.ap[:, 1:2]), reads=[x2], writes=[outb, stat], cost=1100)
                ln_stats(sc, stat, DM, epsc)
                sc.add("g", lambda e, x2=x2: e.tensor_scalar(
                    outb.ap, x2.ap, stat.ap[:, 2:3], stat.ap[:, 4:5],
                    ALU.add, ALU.mult), reads=[x2, stat], writes=[outb], cost=1100)
                dma(h["out"][ti * TA:(ti + 1) * TA, :], outb.ap,
                    reads=[outb], writes=[outB])

        import os
        with nc.Block() as block:
            sc.emit(block, sems)
            if os.environ.get("K_DEBUG"):
                print("sched est span: %.2f ms" % (sc.est_span / 1e6))
            def final_section(eng):
                for (dk, val) in sc.final_dma:
                    eng.wait_ge(sems[dk], val)
            block.sync(final_section)
    return nc


_NC_CACHE = [None]


def kernel(**inputs):
    import ml_dtypes
    f32 = np.float32
    x = np.ascontiguousarray(np.asarray(inputs["L0_attn_x"], dtype=f32))

    def t(a):
        return np.ascontiguousarray(np.asarray(a, dtype=f32).T)

    def col8(v):
        return np.ascontiguousarray(np.asarray(v, dtype=f32).reshape(NDC, P).T)

    A8 = np.ascontiguousarray(
        np.asarray(inputs["L0_attn_A_log"], dtype=f32).reshape(NDC, P, NST)
        .transpose(1, 0, 2).reshape(P, NDC * NST))
    base = dict(
        wpT=t(inputs["L0_attn_proj_W"]), wfT=t(inputs["L0_attn_fconv_W"]),
        wbT=t(inputs["L0_attn_bconv_W"]), wdbcT=t(inputs["L0_attn_deltaBC_W"]),
        wdtT=t(inputs["L0_attn_dt_W"]), wgT=t(inputs["L0_ffn_w_gate"]),
        wuT=t(inputs["L0_ffn_w_up"]), wdT=t(inputs["L0_ffn_w_down"]),
        A_log8=A8,
        gamma8=col8(inputs["L0_attn_ln_w"]), beta8=col8(inputs["L0_attn_ln_b"]),
        bp8=col8(inputs["L0_attn_proj_b"]), bf8=col8(inputs["L0_attn_fconv_b"]),
        bb8=col8(inputs["L0_attn_bconv_b"]), dtb8=col8(inputs["L0_attn_dt_b"]),
        D8=col8(inputs["L0_attn_D"]),
        ident=np.eye(P, dtype=f32),
        identb=np.eye(P, dtype=ml_dtypes.bfloat16),
        ones=np.ones((P, P), dtype=f32),
    )
    in_maps = []
    for core in range(8):
        m = dict(base)
        m["x"] = np.ascontiguousarray(x[core * SPC:(core + 1) * SPC])
        in_maps.append(m)

    if _NC_CACHE[0] is None:
        _NC_CACHE[0] = build_nc()
    res = run_bass_kernel_spmd(_NC_CACHE[0], in_maps, core_ids=list(range(8)))
    out = np.concatenate([r["out"].reshape(SPC, SEQ, DM) for r in res.results], axis=0)
    return np.ascontiguousarray(out.astype(np.float32))
